# version 22
# baseline (speedup 1.0000x reference)
import numpy as np
import ml_dtypes
from contextlib import ExitStack

import os
import concourse.bass as bass
import concourse.mybir as mybir
from concourse.bass_utils import run_bass_kernel_spmd

F32 = mybir.dt.float32
BF16 = mybir.dt.bfloat16
I32 = mybir.dt.int32
AF = mybir.ActivationFunctionType
ALU = mybir.AluOpType
AX = mybir.AxisListType

B, L, E = 4, 4096, 1024
H, D = 16, 64
F = 4096
T = 2048
NT = 16
EC = 8
FC = 32
WJ = 144
SEC = 512
NSEC = 4
ENGS = ("sync", "gpsimd", "tensor", "vector", "scalar")

_CACHE = {}
LAST_DBG = None


class Sem:
    def __init__(self, ctx, nc, name):
        self.h = ctx.enter_context(nc.semaphore(name))
        self.n = 0
        self.name = name


class VSem:

    def __init__(self, ctx, nc, name):
        self.subs = [Sem(ctx, nc, name + "a"), Sem(ctx, nc, name + "b")]
        self.n = 0
        self.name = name

    def pick(self):
        j = self.n // 16
        return self.subs[j % 2]

    @staticmethod
    def xlate(s, v):
        if not isinstance(s, VSem):
            return (s, v)
        jd = v // 16
        if jd <= 0:
            return (s.subs[0], 0)
        return (s.subs[(jd - 1) % 2], 16 * ((jd + 1) // 2))


class Sched:
    def __init__(self):
        self.ops = {e: [] for e in ENGS}
        self.stopped = False

    def emit(self, eng, fn, waits=(), inc=None, k=1, is_dma=False, raw=False):
        if self.stopped:
            return
        w = tuple(VSem.xlate(s, v) for (s, v) in waits if v > 0)
        w = tuple((s, v) for (s, v) in w if v > 0)
        if isinstance(inc, VSem):
            real = inc.pick()
            inc.n += k
            inc = real
        self.ops[eng].append((w, fn, inc, k, is_dma or raw))
        if inc is not None:
            inc.n += k

    def replay(self, blk):
        for eng in ENGS:
            ops = self.ops[eng]

            def mk(ops=ops, eng=eng):
                def body(h):
                    water = {}
                    for (waits, fn, inc, k, is_dma) in ops:
                        for (s, v) in waits:
                            if water.get(s.name, 0) < v:
                                h.wait_ge(s.h, v)
                                water[s.name] = v
                        ins = fn(h)
                        if inc is not None:
                            if not is_dma and eng in ("vector", "scalar",
                                                      "gpsimd"):
                                h.drain(fusable=False).then_inc(inc.h, k)
                            else:
                                ins.then_inc(inc.h, k)
                return body
            getattr(blk, eng)(mk())


def build_nc():
    nc = bass.Bass("TRN2", target_bir_lowering=False)

    xm = nc.dram_tensor("xm", [T, E], F32, kind="ExternalInput").ap()
    xh = nc.dram_tensor("xh", [16, E], F32, kind="ExternalInput").ap()
    mskin = nc.dram_tensor("mskin", [128, 16], F32, kind="ExternalInput").ap()
    Wq = nc.dram_tensor("Wq", [E, E], F32, kind="ExternalInput").ap()
    Wkv = nc.dram_tensor("Wkv", [E, 2 * E], F32, kind="ExternalInput").ap()
    Wwin = nc.dram_tensor("Wwin", [E, 2 * H], F32, kind="ExternalInput").ap()
    Wout = nc.dram_tensor("Wout", [E, E], F32, kind="ExternalInput").ap()
    w1 = nc.dram_tensor("w1", [E, F], F32, kind="ExternalInput").ap()
    w2 = nc.dram_tensor("w2", [F, E], F32, kind="ExternalInput").ap()
    qg2 = nc.dram_tensor("qg2", [128, 1], F32, kind="ExternalInput").ap()
    qb2 = nc.dram_tensor("qb2", [128, 1], F32, kind="ExternalInput").ap()
    kg2 = nc.dram_tensor("kg2", [128, 1], F32, kind="ExternalInput").ap()
    kb2 = nc.dram_tensor("kb2", [128, 1], F32, kind="ExternalInput").ap()
    b1c = nc.dram_tensor("b1c", [128, FC], F32, kind="ExternalInput").ap()
    bwin1 = nc.dram_tensor("bwin1", [128, 2 * H], F32, kind="ExternalInput").ap()
    b2r = nc.dram_tensor("b2r", [128, E], F32, kind="ExternalInput").ap()
    out = nc.dram_tensor("out", [T, E], F32, kind="ExternalOutput").ap()
    dbg = nc.dram_tensor("dbg", [128, 17408], F32, kind="ExternalOutput").ap()

    w1b = nc.dram_tensor("w1b", [E, F], BF16, kind="Internal").ap()
    w2b = nc.dram_tensor("w2b", [F, E], BF16, kind="Internal").ap()

    ident = nc.inline_tensor(np.eye(128, dtype=ml_dtypes.bfloat16), name="ident").ap()
    rmv = np.abs(np.arange(WJ, dtype=np.float32)[None, :]
                 - (np.arange(128, dtype=np.float32)[:, None] + 8.0))
    Rmc = nc.inline_tensor(np.ascontiguousarray(rmv.astype(ml_dtypes.bfloat16)), name="Rmc").ap()
    c025 = nc.inline_tensor(np.full((128, 1), 0.25, np.float32), name="c025").ap()
    cfl = nc.inline_tensor(np.full((128, 1), -63.0 / 128.0, np.float32), name="cfl").ap()
    ceps = nc.inline_tensor(np.full((128, 1), 1e-5, np.float32), name="ceps").ap()

    ctx = ExitStack()
    with ctx:
        def sb(name, shape, dt):
            return ctx.enter_context(nc.sbuf_tensor(name, shape, dt))

        def psalloc(name, shape, dt):
            return ctx.enter_context(nc.psum_tensor(name, shape, dt))

        g1 = sb("g1", [128, 16384], BF16)
        g2 = sb("g2", [128, 16384], BF16)
        g3 = sb("g3", [128, 16384], BF16)
        g4 = sb("g4", [128, 17408], BF16)
        g5 = sb("g5", [128, 16512], BF16)
        g6 = sb("g6", [128, 8192], BF16)
        sm1 = sb("sm1", [128, 2048], F32)

        cR = sb("cR", [128, WJ], BF16)
        cI = sb("cI", [128, 128], BF16)
        c25 = sb("c25", [128, 1], F32)
        c63 = sb("c63", [128, 1], F32)
        cep = sb("cep", [128, 1], F32)
        qgs = sb("qgs", [128, 1], F32)
        qbs = sb("qbs", [128, 1], F32)
        kgs = sb("kgs", [128, 1], F32)
        kbs = sb("kbs", [128, 1], F32)
        b1s = sb("b1s", [128, FC], F32)
        bwb = sb("bwb", [128, 2 * H], F32)
        mskt = sb("mskt", [128, 16], F32)
        wwinb = sb("wwinb", [128, EC * 2 * H], BF16)
        xnhf = sb("xnhf", [128, EC * 16], BF16)
        width = sb("width", [128, NT * H], F32)
        nshp = sb("nshp", [128, NT * H], F32)
        stat = sb("stat", [128, 128], F32)
        spc = sb("spc", [128, 128], F32)
        st16 = sb("st16", [128, 192], F32)
        xr = sb("xr", [128, 2 * E], F32)
        xnr = sb("xnr", [128, 2 * E], BF16)
        htmp = sb("htmp", [128, 128], BF16)
        wrm = sb("wrm", [128, 8], F32)

        g1f = g1[:].bitcast(F32)
        g4f = g4[:].bitcast(F32)
        g4i = g4[:].bitcast(I32)
        hx = sm1[0:16, 0:E]
        hs = sm1[0:16, 0:E]
        hsq = sm1[0:16, E:2 * E]
        hb16 = xnr[0:16, 0:E]
        wpt = sm1[:, 1024:1088]
        hst = stat[0:16, 16:32]
        g6f = g6[:].bitcast(F32)
        xet = g6f[:, 0:2 * SEC]
        att = g6f[:, 2 * SEC:4 * SEC]
        ott = g6f[:, 4 * SEC:6 * SEC]
        b2b = g6f[:, 3072:4096]
        sm1b = sm1[:].bitcast(BF16)

        PA = psalloc("PA", [128, 2048], F32)
        PT = psalloc("PT", [128, 2048], BF16)
        PB = psalloc("PB", [128, 1024], F32)

        dS = VSem(ctx, nc, "dS")
        dG = VSem(ctx, nc, "dG")
        sT = Sem(ctx, nc, "sT")
        sV = Sem(ctx, nc, "sV")
        sS = Sem(ctx, nc, "sS")
        sG = Sem(ctx, nc, "sG")
        ALLS = [dS, dG, sT, sV, sS, sG]

        nc.const_aps.register(F32, -63.0 / 128.0, c63[:, 0:1])
        K = Sched()
        _PH = os.environ.get("KPH", "Z")
        _DBG = os.environ.get("KDBG", "")

        def dbg_dump(tag, srcap, cast=True):
            if _DBG != tag:
                return
            n = srcap.shape[-1]
            w = [(s, s.n) for s in ALLS]
            if cast:
                dma("gpsimd", dG, dbg[0:srcap.shape[0], 0:n], srcap, waits=w)
            else:
                dma("sync", dS, dbg[0:srcap.shape[0], 0:n], srcap, waits=w)

        def dbg_dump_v(tag, srcap):
            if _DBG != tag:
                return
            n = srcap.shape[-1]
            w = [(s, s.n) for s in ALLS]
            for c0 in range(0, n, 2048):
                cn = min(2048, n - c0)
                vo("vector", "tensor_copy", sm1[0:srcap.shape[0], 0:cn],
                   srcap[:, c0:c0 + cn], waits=w, inc=sV)
                dma("sync", dS, dbg[0:srcap.shape[0], c0:c0 + cn],
                    sm1[0:srcap.shape[0], 0:cn], waits=[(sV, sV.n)])
                w = [(dS, dS.n)]

        def wait_alldma():
            for vs in (dS, dG):
                for sub in vs.subs:
                    K.emit("sync", lambda h, s=sub: h.wait_ge(s.h, s.n))

        def phase_end(tag):
            if _PH == tag:
                wait_alldma()
                K.stopped = True

        def dma(eng, sem, outap, inap, waits=()):
            K.emit(eng, lambda h, o=outap, i=inap: h.dma_start(out=o, in_=i),
                   waits=waits, inc=sem, k=16, is_dma=True)

        def mm(outap, lhs, rhs, start, stop, waits=(), inc=None):
            K.emit("tensor", lambda h, o=outap, a=lhs, b=rhs, st=start, sp=stop:
                   h.matmul(o, a, b, start=st, stop=sp), waits=waits, inc=inc)

        def tr(outap, inap, nparts=128, waits=(), inc=None):
            idn = cI[0:nparts, 0:nparts]
            K.emit("tensor", lambda h, o=outap, i=inap, d=idn:
                   h.transpose(o, i, d), waits=waits, inc=inc)

        def act(eng, outap, inap, func, bias=0.0, scale=1.0, waits=(), inc=None):
            K.emit(eng, lambda h, o=outap, i=inap, f=func, b=bias, s=scale:
                   h.activation(o, i, f, bias=b, scale=s), waits=waits, inc=inc)

        def vo(eng, name, *args, waits=(), inc=None, raw=False, **kw):
            K.emit(eng, lambda h, a=args, k2=kw, n=name: getattr(h, n)(*a, **k2),
                   waits=waits, inc=inc, raw=raw)

        def vspace():
            vo("vector", "tensor_copy", spc[:, 0:128], spc[:, 0:128])

        def fence():
            for e, s in (("tensor", sT), ("vector", sV), ("scalar", sS),
                         ("gpsimd", sG)):
                K.emit(e, lambda h: h.drain(), inc=s)
            return [(s, s.n) for s in ALLS]

        def xnf(kc, c0, n):
            return g1[:, kc * 2048 + c0: kc * 2048 + c0 + n]

        def qfv(kc, p0, pn, c0, n):
            return g2[p0:p0 + pn, kc * 2048 + c0: kc * 2048 + c0 + n]

        def kfv(kc, p0, pn, c0, n):
            return g5[p0:p0 + pn, kc * 2064 + c0: kc * 2064 + c0 + n]

        def vshv(p0, pn, jt, c0, n):
            return g4[p0:p0 + pn, jt * 1024 + c0: jt * 1024 + c0 + n]

        def ofv(kc, c0, n):
            return g3[:, kc * 2048 + c0: kc * 2048 + c0 + n]

        dma("sync", dS, cR[:], Rmc)
        dma("sync", dS, cI[:], ident)
        dma("sync", dS, c25[:], c025)
        dma("sync", dS, c63[:], cfl)
        dma("sync", dS, cep[:], ceps)
        dma("sync", dS, qgs[:], qg2)
        dma("sync", dS, qbs[:], qb2)
        dma("sync", dS, kgs[:], kg2)
        dma("sync", dS, kbs[:], kb2)
        dma("sync", dS, b1s[:], b1c)
        dma("sync", dS, mskt[:], mskin)
        dma("sync", dS, bwb[:], bwin1)
        pre = dS.n
        vo("vector", "tensor_copy", spc[:], cI[:], waits=[(dS, pre)], inc=sV)
        v_spc = sV.n
        for fi, f in enumerate((AF.Sqrt, AF.Silu, AF.Sigmoid, AF.Exp,
                                AF.Gelu, AF.Copy)):
            act("scalar", wrm[:, fi:fi + 1], cep[:], f, waits=[(dS, pre)])
        K.emit("scalar", lambda h: h.drain(fusable=False), inc=sS)
        s_warm = sS.n
        for _ in range(8):
            vo("gpsimd", "tensor_tensor", width[:, 0:128], spc[:], spc[:],
               ALU.mult, waits=[(sS, s_warm), (sV, v_spc)])
        K.emit("gpsimd", lambda h: h.drain(fusable=False), inc=sG)
        g_warm = sG.n
        K.emit("scalar", lambda h: h.nop(), waits=[(sG, g_warm)])
        vo("vector", "tensor_scalar", qgs[:], qgs[:], 0.125, None, ALU.mult,
           waits=[(dS, pre)])
        vo("vector", "tensor_scalar", qbs[:], qbs[:], 0.125, None, ALU.mult)
        vo("vector", "tensor_scalar", b1s[:], b1s[:], 256.0, None, ALU.mult, inc=sV)
        v_consts = sV.n

        dma("gpsimd", dG, g6[:].rearrange("p (kc j) -> p kc j", j=1024),
            Wq.rearrange("(kc p) j -> p kc j", p=128))
        d_wq = dG.n
        dma("gpsimd", dG, w1b, w1)
        dma("gpsimd", dG, w2b, w2)
        dma("gpsimd", dG, wwinb[:].rearrange("p (kc j) -> p kc j", j=2 * H),
            Wwin.rearrange("(kc p) j -> p kc j", p=128))
        d_wwin = dG.n

        phase_end('pre')
        def emit_ln(xtile, sqtile, xntile, np_, scol, extra_wait=(), stt=None,
                    statdep=()):
            stt = stt if stt is not None else stat
            bnp = stt[0:np_, scol:scol + 12]
            mv = stt[0:np_, scol + 16:scol + 18]
            sd = stt[0:np_, scol + 24:scol + 25]
            rs = stt[0:np_, scol + 32:scol + 33]
            n2 = xtile.shape[-1] // 2
            vo("vector", "bn_stats", bnp[:, 0:6], xtile[:, 0:n2],
               waits=list(extra_wait) + list(statdep))
            vo("vector", "bn_stats", bnp[:, 6:12], xtile[:, n2:2 * n2])
            vspace()
            vo("vector", "bn_aggr", mv, bnp, inc=sV)
            act("scalar", sd, mv[:, 1:2], AF.Sqrt, bias=cep[0:np_, :],
                waits=[(sV, sV.n)], inc=sS)
            vo("vector", "reciprocal", rs, sd, waits=[(sS, sS.n)])
            vspace()
            vo("vector", "tensor_scalar", xntile, xtile, mv[:, 0:1], rs,
               ALU.subtract, ALU.mult, inc=sV)
            return sV.n

        def emit_xn_phase(gate):
            xdep = [list(gate), list(gate)]
            tpdep = list(gate)
            xndep = [list(gate), list(gate)]
            statd = [list(gate), list(gate)]
            sqd = list(gate)
            for tt in range(NT):
                sl = tt % 2
                xt = xr[:, sl * E:(sl + 1) * E]
                xnt = xnr[:, sl * E:(sl + 1) * E]
                dma("sync", dS, xt, xm[tt * 128:(tt + 1) * 128, :], waits=xdep[sl])
                dld = dS.n
                sact = emit_ln(xt, sm1[:, 0:E], xnt, 128, sl * 64,
                               extra_wait=[(dS, dld)] + xndep[sl] + sqd,
                               statdep=statd[sl])
                statd[sl] = [(sV, sact)]
                xdep[sl] = [(sV, sact)]
                sqd = []
                for kc in range(EC):
                    tr(PT[:, kc * 128:(kc + 1) * 128],
                       xnt[:, kc * 128:(kc + 1) * 128],
                       waits=([(sV, sact)] + tpdep) if kc == 0 else (), inc=sT)
                ttr = sT.n
                dst = g1[:].rearrange("p (kc t) -> p kc t", t=2048)[
                    :, :, tt * 128:(tt + 1) * 128]
                src = PT[:, 0:1024].rearrange("p (kc t) -> p kc t", t=128)
                vo("vector", "tensor_copy", dst, src, waits=[(sT, ttr)], inc=sV)
                tpdep = [(sV, sV.n)]
                xndep[sl] = [(sV, sV.n)]
            return fence()

        gate0 = [(dS, pre), (sV, v_consts)]
        gateA = emit_xn_phase(gate0)

        dbg_dump('xnf', g1[:])
        dbg_dump('xnr', xnr[:])
        dbg_dump_v('xnrv', xnr[:])
        dbg_dump('xr', xr[:], cast=False)
        dbg_dump('sq', sm1[:, 0:E], cast=False)
        dbg_dump('stat', stat[:], cast=False)
        phase_end('A')
        dma("sync", dS, hx, xh)
        dhx = dS.n
        hact = emit_ln(hx, hsq, hb16, 16, 0, extra_wait=[(dS, dhx)] + gateA,
                       stt=stat)
        for kc in range(EC):
            tr(PT[:, 1024 + kc * 16:1024 + (kc + 1) * 16],
               hb16[:, kc * 128:(kc + 1) * 128], nparts=16,
               waits=([(sV, hact)] + gateA) if kc == 0 else (), inc=sT)
        vo("vector", "tensor_copy", xnhf[:], PT[:, 1024:1024 + 128],
           waits=[(sT, sT.n)], inc=sV)
        gateA2 = fence()

        phase_end('A2')
        def emit_headln(qs_t, sq_t, src_psum, qn_t, np_, stt, base, silu_w,
                        statdep=()):
            c = [stt[0:np_, base + 16 * i:base + 16 * (i + 1)]
                 for i in range(6)]
            act("scalar", qs_t, src_psum, AF.Silu, waits=silu_w, inc=sS)
            s_silu = sS.n
            qv = qs_t.rearrange("p (h d) -> p h d", d=D)
            vo("vector", "reduce_sum", c[0], qv, AX.X,
               waits=[(sS, s_silu)] + list(statdep))
            vo("vector", "tensor_tensor", sq_t, qs_t, qs_t, ALU.mult)
            vo("vector", "reduce_sum", c[1],
               sq_t.rearrange("p (h d) -> p h d", d=D), AX.X)
            vo("vector", "tensor_scalar", c[2], c[0], 1.0 / D, None, ALU.mult)
            vspace()
            vo("vector", "tensor_tensor", c[3], c[2], c[2], ALU.mult)
            vo("vector", "tensor_scalar", c[4], c[1], 1.0 / D, None, ALU.mult)
            vspace()
            vo("vector", "tensor_tensor", c[1], c[4], c[3], ALU.subtract,
               inc=sV)
            act("scalar", c[5], c[1], AF.Sqrt, bias=cep[0:np_, :],
                waits=[(sV, sV.n)], inc=sS)
            s_sq = sS.n
            vo("vector", "reciprocal", c[1], c[5], waits=[(sS, s_sq)])
            vspace()
            for h in range(H):
                vo("vector", "tensor_scalar",
                   qn_t[:, h * D:(h + 1) * D], qs_t[:, h * D:(h + 1) * D],
                   c[2][:, h:h + 1], c[1][:, h:h + 1], ALU.subtract, ALU.mult,
                   inc=(sV if h == H - 1 else None))
            return sV.n, s_silu, s_sq

        def emit_proj(gate, wdep0, kind, gains=None):
            pdep = [list(gate), list(gate)]
            qsdep = list(gate)
            pend_tr = None
            qndep = [list(gate), list(gate)]
            tpdep = list(gate)
            statd = [list(gate), list(gate)]
            vdep = [list(gate), list(gate)]
            for tt in range(NT):
                sl = tt % 2
                pt = PA[:, sl * 1024:(sl + 1) * 1024]
                for kc in range(EC):
                    for s in range(2):
                        mm(pt[:, s * 512:(s + 1) * 512],
                           xnf(kc, tt * 128, 128),
                           g6[:, kc * 1024 + s * 512: kc * 1024 + (s + 1) * 512],
                           start=(kc == 0), stop=(kc == EC - 1),
                           waits=(pdep[sl] + wdep0) if (kc == 0 and s == 0) else (),
                           inc=(sT if (kc == EC - 1 and s == 1) else None))
                tmm = sT.n
                if kind == "v":
                    vt = xnr[:, sl * 1024:(sl + 1) * 1024]
                    act("scalar", vt, pt, AF.Silu,
                        waits=[(sT, tmm)] + vdep[sl], inc=sS)
                    s_ev = sS.n
                    pdep[sl] = [(sS, s_ev)]
                    dma("sync", dS, vshv(8, 120, tt, 0, 1024), vt[0:120, :],
                        waits=[(sS, s_ev)])
                    dma("sync", dS, vshv(0, 8, tt + 1, 0, 1024), vt[120:128, :])
                    vdep[sl] = [(dS, dS.n)]
                    continue
                qs_t = sm1[:, 0:E]
                sq_t = sm1[:, E:2 * E]
                qn_t = xnr[:, sl * E:(sl + 1) * E]
                vln, s_silu, s_sq = emit_headln(
                    qs_t, sq_t, pt, qn_t, 128, st16, (tt % 2) * 96,
                    silu_w=[(sT, tmm)] + qsdep + qndep[sl], statdep=statd[sl])
                statd[sl] = [(sS, s_sq)]
                qsdep = [(sV, vln)]
                pdep[sl] = [(sS, s_silu)]

                def emit_tr(tt=tt, vln=vln, qn_t=qn_t, sl=sl):
                    nonlocal tpdep
                    for kc in range(EC):
                        tr(PT[:, kc * 128:(kc + 1) * 128],
                           qn_t[:, kc * 128:(kc + 1) * 128],
                           waits=([(sV, vln)] + tpdep) if kc == 0 else (),
                           inc=sT)
                    ttr = sT.n
                    gg, bb = gains
                    if kind == "q":
                        dst = g2[:].rearrange("p (kc t) -> p kc t", t=2048)[
                            :, :, tt * 128:(tt + 1) * 128]
                    else:
                        dst = g5[:].rearrange("p (kc t) -> p kc t", t=2064)[
                            :, :, 8 + tt * 128:8 + (tt + 1) * 128]
                    srcp = PT[:, 0:1024].rearrange("p (kc t) -> p kc t", t=128)
                    vo("vector", "tensor_scalar", dst, srcp, gg, bb,
                       ALU.mult, ALU.add, waits=[(sT, ttr)], inc=sV)
                    tpdep = [(sV, sV.n)]
                    qndep[sl] = [(sV, sV.n)]
                if pend_tr is not None:
                    pend_tr()
                pend_tr = emit_tr
            if pend_tr is not None:
                pend_tr()
            return fence()

        gateB1 = emit_proj(gateA2, [(dG, d_wq)], "q", gains=(qgs[:], qbs[:]))
        phase_end('B1')
        dma("gpsimd", dG, g6[:].rearrange("p (kc j) -> p kc j", j=1024),
            Wkv.rearrange("(kc p) j -> p kc j", p=128)[:, :, 0:1024],
            waits=gateB1)
        gateB2a = emit_proj(gateB1, [(dG, dG.n)], "k", gains=(kgs[:], kbs[:]))
        phase_end('B2a')
        dma("gpsimd", dG, g6[:].rearrange("p (kc j) -> p kc j", j=1024),
            Wkv.rearrange("(kc p) j -> p kc j", p=128)[:, :, 1024:2048],
            waits=gateB2a)
        gateB2b = emit_proj(gateB2a, [(dG, dG.n)], "v")

        phase_end('B2b')
        pth = PA[0:16, 0:1024]
        for kc in range(EC):
            for s in range(2):
                mm(pth[:, s * 512:(s + 1) * 512], xnhf[:, kc * 16:(kc + 1) * 16],
                   g6[:, kc * 1024 + s * 512: kc * 1024 + (s + 1) * 512],
                   start=(kc == 0), stop=(kc == EC - 1),
                   waits=(gateB2b if (kc == 0 and s == 0) else ()),
                   inc=(sT if (kc == EC - 1 and s == 1) else None))
        thv = sT.n
        act("scalar", xnr[0:16, 0:1024], pth[0:16, :], AF.Silu,
            waits=[(sT, thv)], inc=sS)
        dma("sync", dS, vshv(0, 8, 0, 0, 1024), xnr[0:8, 0:1024],
            waits=[(sS, sS.n)])
        dma("sync", dS, vshv(8, 8, 16, 0, 1024), xnr[8:16, 0:1024])
        gateB4v = fence()

        phase_end('B4v')
        dma("gpsimd", dG, g6[:].rearrange("p (kc j) -> p kc j", j=1024),
            Wkv.rearrange("(kc p) j -> p kc j", p=128)[:, :, 0:1024],
            waits=gateB4v)
        d_wk2 = dG.n
        for kc in range(EC):
            for s in range(2):
                mm(pth[:, s * 512:(s + 1) * 512], xnhf[:, kc * 16:(kc + 1) * 16],
                   g6[:, kc * 1024 + s * 512: kc * 1024 + (s + 1) * 512],
                   start=(kc == 0), stop=(kc == EC - 1),
                   waits=([(dG, d_wk2)] if (kc == 0 and s == 0) else ()),
                   inc=(sT if (kc == EC - 1 and s == 1) else None))
        thk = sT.n
        vlnh, _s1, _s2 = emit_headln(hs, hsq, pth, hb16, 16, st16, 0,
                                     silu_w=[(sT, thk)])
        for kc in range(EC):
            tr(PT[:, 1024 + kc * 16:1024 + (kc + 1) * 16],
               hb16[:, kc * 128:(kc + 1) * 128], nparts=16,
               waits=[(sV, vlnh)] if kc == 0 else (), inc=sT)
        vo("vector", "tensor_scalar", htmp[:], PT[:, 1024:1024 + 128],
           kgs[:], kbs[:], ALU.mult, ALU.add, waits=[(sT, sT.n)], inc=sV)
        msrc = htmp[:].rearrange("p (kc hf r) -> p kc hf r", hf=2, r=8)
        mmask = mskt[:].rearrange("p (hf r) -> p hf r", r=8).unsqueeze(1) \
            .broadcast_to([128, EC, 2, 8])
        kdst = bass.AP(g5[:].tensor, 0,
                       [list(g5[:].ap[0]), [2064, EC], [2056, 2], [1, 8]])
        vo("vector", "tensor_tensor", kdst, msrc, mmask, ALU.mult,
           waits=[(sV, sV.n)], inc=sV)

        phase_end('B4k')
        gateB3 = fence()
        wstt_dep = list(gateB3)
        pdep3 = list(gateB3)
        for tt in range(NT):
            pt = PA[:, 0:32]
            for kc in range(EC):
                mm(pt, xnf(kc, tt * 128, 128), wwinb[:, kc * 32:(kc + 1) * 32],
                   start=(kc == 0), stop=(kc == EC - 1),
                   waits=(pdep3 + [(dG, d_wwin)]) if kc == 0 else (),
                   inc=(sT if kc == EC - 1 else None))
            tmm = sT.n
            wp1 = wpt[:, 0:32]
            wp2 = wpt[:, 32:64]
            vo("vector", "scalar_tensor_tensor", wp1, pt, 1.0, bwb[:],
               ALU.mult, ALU.add, waits=[(sT, tmm)] + wstt_dep, inc=sV)
            pdep3 = [(sV, sV.n)]
            act("scalar", wp2, wp1, AF.Silu, waits=[(sV, sV.n)], inc=sS)
            act("scalar", wp1, wp2, AF.Tanh, scale=0.5, inc=sS)
            sw = sS.n
            vo("vector", "tensor_scalar", width[:, tt * H:(tt + 1) * H],
               wp1[:, 0:16], 4.0, 4.5, ALU.mult, ALU.add, waits=[(sS, sw)])
            vo("vector", "tensor_scalar", nshp[:, tt * H:(tt + 1) * H],
               wp1[:, 16:32], -2.375, -2.625, ALU.mult, ALU.add, inc=sV)
            wstt_dep = [(sV, sV.n)]
        gateB = fence()

        dbg_dump('qf', g2[:])
        dbg_dump('kf', g5[:])
        dbg_dump('vsh', g4[:])
        dbg_dump('wid', width[:], cast=False)
        dbg_dump('nsh', nshp[:], cast=False)
        dbg_dump('xnhf', xnhf[:])
        phase_end('B')
        sarg = g6f[:, 0:1152]
        Lg = g6f[:, 1152:2304]
        pnb = g6[:, 4608:6656]
        ptsb = sm1[:].bitcast(BF16)[:, 0:2048]
        otmp = xnr[:, 0:1024]

        c_sdep = list(gateB)
        c_odep = list(gateB)
        c_ptdep = list(gateB)
        c_otdep = list(gateB)
        c_bufdep = list(gateB)
        c_pndep = list(gateB)
        c_ptsbdep = list(gateB)
        c_otmpdep = list(gateB)
        for tt in range(NT):
            for hg in range(2):
                for hh in range(8):
                    h = hg * 8 + hh
                    vo("vector", "tensor_scalar", Lg[:, hh * WJ:(hh + 1) * WJ],
                       cR[:], width[:, tt * H + h:tt * H + h + 1],
                       nshp[:, tt * H + h:tt * H + h + 1],
                       ALU.subtract, ALU.mult,
                       waits=(c_bufdep if hh == 0 else ()),
                       inc=(sV if hh == 7 else None))
                va = sV.n
                act("scalar", sarg, Lg, AF.Tanh, waits=[(sV, va)], inc=sS)
                ssig = sS.n
                rdep = []
                for rnd in range(4):
                    for k2 in range(2):
                        h = hg * 8 + rnd * 2 + k2
                        kc = h // 2
                        p0 = (h % 2) * 64
                        mm(PA[:, k2 * 512:k2 * 512 + WJ],
                           qfv(kc, p0, 64, tt * 128, 128),
                           kfv(kc, p0, 64, tt * 128, WJ), True, True,
                           waits=(c_sdep + rdep) if k2 == 0 else (),
                           inc=(sT if k2 == 1 else None))
                    tsc = sT.n
                    s_in = PA[:, 0:1024].rearrange("p (k j) -> p k j", j=512)[
                        :, :, 0:WJ]
                    sg_in = sarg.rearrange("p (h j) -> p h j", j=WJ)[
                        :, rnd * 2:rnd * 2 + 2, :]
                    lg_out = Lg.rearrange("p (h j) -> p h j", j=WJ)[
                        :, rnd * 2:rnd * 2 + 2, :]
                    vo("vector", "scalar_tensor_tensor", lg_out, sg_in, 5e3,
                       s_in, ALU.mult, ALU.add, waits=[(sT, tsc), (sS, ssig)],
                       inc=sV, raw=True)
                    rdep = [(sV, sV.n)]
                c_sdep = rdep
                lgv = Lg.rearrange("p (h j) -> p h j", j=WJ)
                sav = sarg.rearrange("p (h j) -> p h j", j=WJ)
                nmx = stat[:, 0:8]
                sme = stat[:, 16:24]
                sme2 = stat[:, 32:40]
                vo("vector", "reduce_max", nmx, lgv, AX.X, negate=True)
                vspace()
                nb = nmx.unsqueeze(2).broadcast_to([128, 8, WJ])
                vo("vector", "tensor_tensor", sav, lgv, nb, ALU.add, inc=sV)
                act("scalar", Lg, sarg, AF.Exp, waits=[(sV, sV.n)], inc=sS)
                sexp = sS.n
                c_bufdep = [(sS, sexp)]
                vo("vector", "reduce_sum", sme, lgv, AX.X, waits=[(sS, sexp)])
                vspace()
                vo("vector", "reciprocal", sme2, sme)
                vspace()
                rb = sme2.unsqueeze(2).broadcast_to([128, 8, WJ])
                pn_out = pnb.rearrange("p (h j) -> p h j", j=256)[:, :, 0:WJ]
                vo("vector", "tensor_tensor", pn_out, lgv, rb, ALU.mult,
                   waits=c_pndep, inc=sV)
                vpn = sV.n
                for g in range(16):
                    tr(PT[:, g * 128:(g + 1) * 128],
                       pnb[:, g * 128:(g + 1) * 128],
                       waits=([(sV, vpn)] + c_ptdep) if g == 0 else (), inc=sT)
                tpt = sT.n
                act("scalar", ptsb, PT[:, 0:2048], AF.Copy,
                    waits=[(sT, tpt)] + c_ptsbdep, inc=sS)
                spt = sS.n
                c_ptdep = [(sS, spt)]
                c_pndep = [(sS, spt)]
                for hh in range(8):
                    h = hg * 8 + hh
                    o_out = PA[:, 1024 + h * 64:1024 + (h + 1) * 64]
                    mm(o_out, ptsb[:, (2 * hh) * 128:(2 * hh + 1) * 128],
                       vshv(0, 128, tt, h * 64, 64), True, False,
                       waits=([(sS, spt)] +
                              (c_odep if (hh == 0 and hg == 0) else [])))
                    mm(o_out, ptsb[0:16, (2 * hh + 1) * 128:(2 * hh + 2) * 128],
                       vshv(0, 16, tt + 1, h * 64, 64), False, True,
                       inc=(sT if hh == 7 else None))
                tav = sT.n
                vo("vector", "tensor_copy", otmp[:, hg * 512:(hg + 1) * 512],
                   PA[:, 1024 + hg * 512:1024 + (hg + 1) * 512],
                   waits=[(sT, tav)] + (c_otmpdep if hg == 0 else []), inc=sV,
                   raw=(hg == 0))
                c_ptsbdep = [(sV, sV.n)]
            c_odep = [(sV, sV.n)]
            voe = sV.n
            for kc in range(EC):
                tr(PT[:, kc * 128:(kc + 1) * 128],
                   otmp[:, kc * 128:(kc + 1) * 128],
                   waits=([(sV, voe)] + c_otdep + c_ptdep) if kc == 0 else (),
                   inc=sT)
            tot = sT.n
            dst = g3[:].rearrange("p (kc t) -> p kc t", t=2048)[
                :, :, tt * 128:(tt + 1) * 128]
            src = PT[:, 0:1024].rearrange("p (kc t) -> p kc t", t=128)
            act("scalar", dst, src, AF.Copy, waits=[(sT, tot)], inc=sS)
            c_otdep = [(sS, sS.n)]
            c_ptdep = [(sS, sS.n)]
            c_otmpdep = [(sS, sS.n)]
        gateC = fence()

        dbg_dump('of', g3[:])
        phase_end('C')
        dma("gpsimd", dG, g5[:, 0:8192].rearrange("p (kc j) -> p kc j", j=1024),
            Wout.rearrange("(kc p) j -> p kc j", p=128), waits=gateC)
        d_wo = dG.n
        dma("sync", dS, b2b, b2r, waits=gateC)
        vo("vector", "tensor_scalar", b2b, b2b, 256.0, None, ALU.mult,
           waits=[(dS, dS.n)] + gateC, inc=sV)
        gateE0 = gateC + [(sV, sV.n), (dS, dS.n), (dG, dG.n)]

        def slot(i, int_=False):
            return (g4i if int_ else g4f)[:, i * 512:(i + 1) * 512]
        g5f = g5[:].bitcast(F32)
        g5i = g5[:].bitcast(I32)

        def slot5(j, int_=False):
            v = g5i if int_ else g5f
            return v[:, 4096 + j * 512: 4096 + (j + 1) * 512]
        bHq3 = [slot(0, True), slot(1, True), slot(2, True)]
        bm3 = [slot(3, True), slot(4, True), slot(5, True)]
        bglo3 = [slot(6), slot(7), slot(8)]
        bghi3 = [slot(9), slot(10), slot(11)]
        bdg3 = [slot(14), slot(15), slot(16)]
        bby2 = [slot5(0), slot5(1)]
        bYq3 = [slot5(2, True), slot5(3, True), slot5(7, True)]
        bii = slot5(4, True)
        btd = slot5(5)
        bmf3 = [slot(12), slot(13), slot5(6)]
        xnri = xnr[:].bitcast(I32)
        bYo2 = [xnri[:, 0:512], xnri[:, 512:1024]]

        def w1sb(sl):
            return sm1b[:, sl * 1024:(sl + 1) * 1024]

        def w2sb(s4):
            return sm1b[:, 2048 + s4 * 512: 2048 + (s4 + 1) * 512]

        e_hdep = [list(gateE0), list(gateE0)]
        e_w1dep = [list(gateE0), list(gateE0)]
        e_w2dep = [list(gateE0) for _ in range(4)]
        e_accdep = list(gateE0)
        e_wodep = [list(gateE0), list(gateE0)]
        e_xedep = [list(gateE0), list(gateE0)]
        e_otdep = [list(gateE0), list(gateE0)]
        e_atdep = [list(gateE0), list(gateE0)]
        sEnd3 = [list(gateE0) for _ in range(3)]
        gEnd3 = [list(gateE0) for _ in range(3)]
        btfEnd3 = [list(gateE0) for _ in range(3)]
        g2End3 = [list(gateE0) for _ in range(3)]
        yoEnd2 = [list(gateE0), list(gateE0)]
        e_w2tok = [list(gateE0) for _ in range(FC)]
        d_g2_tok = [None] * FC
        vyq_tok = [None] * FC
        for sec in range(NSEC):
            for dh in range(2):
                pend_g2 = {}
                pend_w2 = {}
                pend_td = {}
                for fc in range(FC):
                    sl = fc % 2
                    p3 = fc % 3
                    if dh == 0:
                        w1in = bass.AP(w1b.tensor, fc * 128,
                                       [[F, 128], [128 * F, EC], [1, 128]])
                        dma("sync", dS,
                            w1sb(sl).rearrange("p (kc j) -> p kc j", j=128),
                            w1in, waits=e_w1dep[sl])
                        dw1 = dS.n
                        hp = PB[:, sl * 512:(sl + 1) * 512]
                        for kc in range(EC):
                            mm(hp, w1sb(sl)[:, kc * 128:(kc + 1) * 128],
                               xnf(kc, sec * SEC, SEC),
                               start=(kc == 0), stop=(kc == EC - 1),
                               waits=([(dS, dw1)] + e_hdep[sl]) if kc == 0 else (),
                               inc=(sT if kc == EC - 1 else None))
                        e_w1dep[sl] = [(sT, sT.n)]
                        th = sT.n
                        vo("vector", "tensor_scalar", bHq3[p3], hp, 256.0,
                           b1s[:, fc:fc + 1], ALU.mult, ALU.add,
                           waits=[(sT, th)] + sEnd3[p3])
                        vo("vector", "tensor_scalar", bm3[p3], bHq3[p3], 63,
                           None, ALU.bitwise_and, inc=sV)
                        vm = sV.n
                        e_hdep[sl] = [(sV, vm)]
                        if fc - 2 in pend_td:
                            pend_td.pop(fc - 2)()
                        if fc - 3 in pend_g2:
                            pend_g2.pop(fc - 3)()
                        act("scalar", bii, bHq3[p3], AF.Copy,
                            bias=-63.0 / 128.0, scale=1.0 / 64.0,
                            waits=[(sV, vm)])
                        act("scalar", bmf3[p3], bm3[p3], AF.Copy,
                            scale=1.0 / 64.0, waits=btfEnd3[p3])
                        act("scalar", bglo3[p3], bii, AF.Gelu, scale=0.25,
                            waits=gEnd3[p3])
                        act("scalar", bghi3[p3], bii, AF.Gelu, bias=c25[:],
                            scale=0.25, inc=sS)
                        st = sS.n
                        sEnd3[p3] = [(sS, st)]
                        vo("gpsimd", "tensor_tensor", bdg3[p3], bghi3[p3],
                           bglo3[p3], ALU.subtract, waits=[(sS, st)], inc=sG)
                        gdg = sG.n

                        def mk_td(fc=fc, sl=sl, p3=p3, gdg=gdg):
                            def go():
                                vo("vector", "tensor_tensor", btd, bmf3[p3],
                                   bdg3[p3], ALU.mult, waits=[(sG, gdg)])
                                vo("vector", "tensor_tensor", bby2[sl], btd,
                                   bglo3[p3], ALU.add)
                                vo("vector", "tensor_scalar", bYq3[p3],
                                   bby2[sl], 256.0, None, ALU.mult,
                                   waits=g2End3[p3], inc=sV)
                                vyq_tok[fc] = sV.n
                                btfEnd3[p3] = [(sV, sV.n)]
                                gEnd3[p3] = [(sV, sV.n)]
                            return go
                        pend_td[fc] = mk_td()

                        def mk_g2(fc=fc, p3=p3):
                            def go():
                                dma("gpsimd", dG,
                                    g2[:, fc * 512:(fc + 1) * 512], bYq3[p3],
                                    waits=[(sV, vyq_tok[fc])] + e_w2tok[fc])
                                d_g2_tok[fc] = dG.n
                                g2End3[p3] = [(dG, dG.n)]
                            return go
                        pend_g2[fc] = mk_g2()

                    def mk_w2(fc=fc, dh=dh):
                        s4 = fc % 4

                        def go():
                            dma("sync", dS,
                                w2sb(s4), w2b[fc * 128:(fc + 1) * 128,
                                              dh * 512:(dh + 1) * 512],
                                waits=e_w2dep[s4])
                            dw2 = dS.n
                            for r in range(4):
                                mm(PA[:, r * 512:(r + 1) * 512],
                                   g2[:, fc * 512 + r * 128:
                                      fc * 512 + (r + 1) * 128],
                                   w2sb(s4),
                                   start=(fc == 0), stop=(fc == FC - 1),
                                   waits=([(dS, dw2),
                                           (dG, d_g2_tok[fc])] +
                                          (e_accdep if (fc == 0 and r == 0)
                                           else [])) if r == 0 else (),
                                   inc=(sT if r == 3 else None))
                            e_w2dep[s4] = [(sT, sT.n)]
                            e_w2tok[fc] = [(sT, sT.n)]
                        return go
                    if dh == 0:
                        pend_w2[fc] = mk_w2()
                        if fc - 4 in pend_w2:
                            pend_w2.pop(fc - 4)()
                    else:
                        mk_w2()()
                if dh == 0:
                    for fc2 in range(FC - 4, FC):
                        if fc2 in pend_td:
                            pend_td.pop(fc2)()
                        if fc2 in pend_g2:
                            pend_g2.pop(fc2)()
                        if fc2 in pend_w2:
                            pend_w2.pop(fc2)()
                tw2 = sT.n
                e_wodep[0] = e_wodep[0] + [(sV, sV.n)]
                e_wodep[1] = e_wodep[1] + [(sV, sV.n)]
                for r in range(4):
                    sl = r % 2
                    wo = PB[:, sl * 512:(sl + 1) * 512]
                    for kc in range(EC):
                        mm(wo, ofv(kc, sec * SEC + r * 128, 128),
                           g5[:, kc * 1024 + dh * 512:
                              kc * 1024 + (dh + 1) * 512],
                           start=(kc == 0), stop=(kc == EC - 1),
                           waits=([(dG, d_wo)] + e_wodep[sl]) if kc == 0
                           else (),
                           inc=(sT if kc == EC - 1 else None))
                    two = sT.n
                    a_t = att[:, sl * SEC:(sl + 1) * SEC]
                    act("scalar", a_t, wo, AF.Silu,
                        waits=[(sT, two)] + e_atdep[sl], inc=sS)
                    sat = sS.n
                    e_wodep[sl] = [(sS, sat)]
                    x_t = xet[:, sl * SEC:(sl + 1) * SEC]
                    dma("sync", dS, x_t,
                        xm[sec * SEC + r * 128: sec * SEC + (r + 1) * 128,
                           dh * 512:(dh + 1) * 512], waits=e_xedep[sl])
                    dxe = dS.n
                    o_t = ott[:, sl * SEC:(sl + 1) * SEC]
                    vo("vector", "scalar_tensor_tensor", o_t,
                       PA[:, r * 512:(r + 1) * 512], 1.0,
                       b2b[:, dh * 512:(dh + 1) * 512], ALU.mult, ALU.add,
                       waits=[(sT, tw2)] + e_otdep[sl])
                    vo("vector", "tensor_copy", bYo2[sl], o_t,
                       waits=yoEnd2[sl], inc=sV)
                    vyo = sV.n
                    act("scalar", o_t, bYo2[sl], AF.Copy, scale=1.0 / 256.0,
                        waits=[(sV, vyo)], inc=sS)
                    sof = sS.n
                    yoEnd2[sl] = [(sS, sof)]
                    K.emit("gpsimd", lambda h, a=a_t, x=x_t:
                           h.tensor_tensor(a, a, x, ALU.add),
                           waits=[(dS, dxe), (sS, sat)], inc=sG)
                    ga = sG.n
                    vo("vector", "tensor_tensor", x_t, o_t, a_t, ALU.add,
                       waits=[(sS, sof), (sG, ga)], inc=sV)
                    vout = sV.n
                    e_atdep[sl] = [(sV, vout)]
                    e_otdep[sl] = [(sV, vout)]
                    dma("sync", dS,
                        out[sec * SEC + r * 128: sec * SEC + (r + 1) * 128,
                            dh * 512:(dh + 1) * 512], x_t,
                        waits=[(sV, vout)])
                    e_xedep[sl] = [(dS, dS.n), (sG, ga)]
                e_accdep = [(sV, sV.n)]

        wait_alldma()

        with nc.Block() as blk:
            K.replay(blk)
    return nc


def kernel(L0_attn_x, L0_attn_Wq, L0_attn_Wkv, L0_attn_q_g, L0_attn_q_b,
           L0_attn_k_g, L0_attn_k_b, L0_attn_Wwin, L0_attn_bwin, L0_attn_Wout,
           L0_ffn_w1, L0_ffn_b1, L0_ffn_w2, L0_ffn_b2):
    x = np.ascontiguousarray(np.asarray(L0_attn_x, dtype=np.float32))
    Wqv = np.ascontiguousarray(np.asarray(L0_attn_Wq, dtype=np.float32))
    Wkvv = np.ascontiguousarray(np.asarray(L0_attn_Wkv, dtype=np.float32))
    Wwinv = np.ascontiguousarray(np.asarray(L0_attn_Wwin, dtype=np.float32))
    Woutv = np.ascontiguousarray(np.asarray(L0_attn_Wout, dtype=np.float32))
    w1v = np.ascontiguousarray(np.asarray(L0_ffn_w1, dtype=np.float32))
    w2v = np.ascontiguousarray(np.asarray(L0_ffn_w2, dtype=np.float32))
    qg = np.asarray(L0_attn_q_g, np.float32)
    qb = np.asarray(L0_attn_q_b, np.float32)
    kg = np.asarray(L0_attn_k_g, np.float32)
    kb = np.asarray(L0_attn_k_b, np.float32)
    b1 = np.asarray(L0_ffn_b1, np.float32)
    bwin = np.asarray(L0_attn_bwin, np.float32)
    b2 = np.asarray(L0_ffn_b2, np.float32)

    qg2 = np.ascontiguousarray(np.tile(qg, 2)[:, None])
    qb2 = np.ascontiguousarray(np.tile(qb, 2)[:, None])
    kg2 = np.ascontiguousarray(np.tile(kg, 2)[:, None])
    kb2 = np.ascontiguousarray(np.tile(kb, 2)[:, None])
    b1c = np.ascontiguousarray(b1.reshape(FC, 128).T)
    bwin1 = np.ascontiguousarray(np.broadcast_to(bwin[None, :], (128, 2 * H)))
    b2r = np.ascontiguousarray(np.broadcast_to(b2[None, :], (128, E)))

    if "nc" not in _CACHE:
        _CACHE["nc"] = build_nc()
    nc = _CACHE["nc"]

    in_maps = []
    for c in range(8):
        b_, hc = c // 2, c % 2
        t0 = hc * T
        xmv = np.ascontiguousarray(x[b_, t0:t0 + T, :])
        xhv = np.zeros((16, E), np.float32)
        msk = np.zeros(16, np.float32)
        if t0 - 8 >= 0:
            xhv[0:8] = x[b_, t0 - 8:t0, :]
            msk[0:8] = 1.0
        if t0 + T + 8 <= L:
            xhv[8:16] = x[b_, t0 + T:t0 + T + 8, :]
            msk[8:16] = 1.0
        mskb = np.ascontiguousarray(np.broadcast_to(msk[None, :], (128, 16)))
        in_maps.append({
            "xm": xmv, "xh": xhv, "mskin": mskb,
            "Wq": Wqv, "Wkv": Wkvv, "Wwin": Wwinv, "Wout": Woutv,
            "w1": w1v, "w2": w2v,
            "qg2": qg2, "qb2": qb2, "kg2": kg2, "kb2": kb2,
            "b1c": b1c, "bwin1": bwin1, "b2r": b2r,
        })
    res = run_bass_kernel_spmd(nc, in_maps, core_ids=list(range(8)))
    global LAST_DBG
    LAST_DBG = [np.asarray(res.results[c].get("dbg")) for c in range(8)]
    o = np.empty((B, L, E), np.float32)
    for c in range(8):
        b_, hc = c // 2, c % 2
        o[b_, hc * T:(hc + 1) * T, :] = np.asarray(res.results[c]["out"])
    return o
